# version 07 (final)
# baseline (speedup 1.0000x reference)
import numpy as np
from contextlib import ExitStack

import concourse.bass as bass
import concourse.mybir as mybir
from concourse.bass_utils import run_bass_kernel_spmd

F32 = mybir.dt.float32
BF16 = mybir.dt.bfloat16
I32 = mybir.dt.int32
AF = mybir.ActivationFunctionType
ALU = mybir.AluOpType
AX = mybir.AxisListType

H = 4096
NHT = 32
NH = 32
HD = 128
SB_ = 512
TPC = 1024
NT = 8
NCORE = 8
PW = 256
NPAN = H // PW
RSCALE = 0.08838834764831845
LORA_SCALING = 2.0
EPS_LN = 1.0e-5
EPS_PRIOR = 1.0e-12


class Sem:
    def __init__(self, es, nc, name):
        self.h = es.enter_context(nc.semaphore(name))
        self.n = 0

    def dma(self, instr):
        instr.then_inc(self.h, 16)
        self.n += 16
        return self.n

    def inc(self, instr):
        instr.then_inc(self.h, 1)
        self.n += 1
        return self.n


def build_nc(trunc=99):
    nc = bass.Bass("TRN2", target_bir_lowering=False)

    x_in = nc.dram_tensor("x", [2, SB_, H], F32, kind="ExternalInput")
    qkvw = nc.dram_tensor("qkvw", [H, 3 * H], F32, kind="ExternalInput")
    qkvb = nc.dram_tensor("qkvb", [3 * H], F32, kind="ExternalInput")
    ow = nc.dram_tensor("ow", [H, H], F32, kind="ExternalInput")
    ob = nc.dram_tensor("ob", [H], F32, kind="ExternalInput")
    wb = nc.dram_tensor("wb", [H, H], F32, kind="ExternalInput")
    bb = nc.dram_tensor("bb", [H], F32, kind="ExternalInput")
    la = nc.dram_tensor("la", [16, 8, H], F32, kind="ExternalInput")
    lb = nc.dram_tensor("lb", [16, H, 8], F32, kind="ExternalInput")
    rw = nc.dram_tensor("rw", [16, H], F32, kind="ExternalInput")
    rb = nc.dram_tensor("rb", [16], F32, kind="ExternalInput")
    prior = nc.dram_tensor("prior", [16], F32, kind="ExternalInput")
    active = nc.dram_tensor("active", [4], I32, kind="ExternalInput")
    out = nc.dram_tensor("out", [2, SB_, H], F32, kind="ExternalOutput")

    k_in = nc.dram_tensor("k_in", [H, TPC], BF16)
    v_in = nc.dram_tensor("v_in", [NH, TPC, HD], BF16)
    k_g = nc.dram_tensor("k_g", [NCORE * H, TPC], BF16)
    v_g = nc.dram_tensor("v_g", [NCORE * NH, TPC, HD], BF16)
    q_sp = nc.dram_tensor("q_sp", [H, TPC], BF16)
    xn_sp = nc.dram_tensor("xn_sp", [128, NHT, TPC], BF16)
    if trunc < 99:
        d_q = nc.dram_tensor("d_q", [H, TPC], BF16, kind="ExternalOutput")
        d_k = nc.dram_tensor("d_k", [H, TPC], BF16, kind="ExternalOutput")
        d_v = nc.dram_tensor("d_v", [NH, TPC, HD], BF16, kind="ExternalOutput")
        d_xn = nc.dram_tensor("d_xn", [128, NHT, TPC], BF16, kind="ExternalOutput")
        d_zw = nc.dram_tensor("d_zw", [128, TPC], BF16, kind="ExternalOutput")
        d_alpha = nc.dram_tensor("d_alpha", [128, TPC], BF16, kind="ExternalOutput")
        d_z = nc.dram_tensor("d_z", [128, TPC], BF16, kind="ExternalOutput")
        d_lg = nc.dram_tensor("d_lg", [16, TPC], F32, kind="ExternalOutput")
        d_lg2 = nc.dram_tensor("d_lg2", [32, 64], F32, kind="ExternalOutput")
        d_ctx = nc.dram_tensor("d_ctx", [128, NH, TPC], BF16, kind="ExternalOutput")
        d_bt = nc.dram_tensor("d_bt", [128, H], BF16, kind="ExternalOutput")

    ms = {}
    es = ExitStack()
    with es:
        ident = es.enter_context(nc.sbuf_tensor("ident", [128, 128], BF16))
        ones_row_f = es.enter_context(nc.sbuf_tensor("ones_row_f", [1, 128], F32))
        ones_col_f = es.enter_context(nc.sbuf_tensor("ones_col_f", [128, 1], F32))
        ones_row_bf = es.enter_context(nc.sbuf_tensor("ones_row_bf", [1, 128], BF16))
        ones_col_bf = es.enter_context(nc.sbuf_tensor("ones_col_bf", [128, 1], BF16))
        btT = es.enter_context(nc.sbuf_tensor("btT", [128, H], BF16))
        zw = es.enter_context(nc.sbuf_tensor("zw", [128, TPC], BF16))
        stn = ["m", "A", "B", "V0", "Veps", "s", "r0", "t2", "w", "u", "r", "mneg", "negb"]
        st = {k: es.enter_context(nc.sbuf_tensor("st_" + k, [1, 128], F32)) for k in stn}
        eps12 = es.enter_context(nc.sbuf_tensor("eps12", [128, 1], F32))

        psW0 = es.enter_context(nc.psum_tensor("psW0", [128, 1024], F32))
        psW1 = es.enter_context(nc.psum_tensor("psW1", [128, 1024], F32))
        psA = es.enter_context(nc.psum_tensor("psA", [128, 512], F32))
        psB = es.enter_context(nc.psum_tensor("psB", [128, 512], F32))
        psT = es.enter_context(nc.psum_tensor("psT", [128, 2048], BF16))

        sLx = Sem(es, nc, "sLx")
        sW = Sem(es, nc, "sW")
        sT = Sem(es, nc, "sT")
        sE = Sem(es, nc, "sE")
        sV = Sem(es, nc, "sV")
        sSt = Sem(es, nc, "sSt")
        sCC = Sem(es, nc, "sCC")
        sKV = Sem(es, nc, "sKV")
        sOut = Sem(es, nc, "sOut")

        blk = es.enter_context(nc.Block())

        arA = bass.AutoArena(nc)
        xn_fm = arA.sbuf_tensor("xn_fm", [128, NHT, TPC], BF16)
        arA.allocate()

        arB = bass.AutoArena(nc)
        x_tile = arB.sbuf_tensor("x_tile", [128, H], F32)
        x_bf = arB.sbuf_tensor("x_bf", [128, H], BF16)
        xfm_t = [arB.sbuf_tensor("xfm_t0", [128, NHT, 128], BF16)] * 2
        sq_t = arB.sbuf_tensor("sq_t", [128, NHT, 128], BF16)
        rrep_sb = arB.sbuf_tensor("rrep_sb", [128, 128], F32)
        nrep_sb = arB.sbuf_tensor("nrep_sb", [128, 128], F32)
        wpan = [arB.sbuf_tensor(f"wpan{i}", [128, NHT, PW], BF16) for i in range(2)]
        k_stage = arB.sbuf_tensor("k_stage", [128, 2, TPC], BF16)
        q_stage = arB.sbuf_tensor("q_stage", [128, 2, TPC], BF16)
        v_stage = arB.sbuf_tensor("v_stage", [128, NT, PW], BF16)
        qkvb_fm = arB.sbuf_tensor("qkvb_fm", [128, 64], F32)
        qkvb_v = arB.sbuf_tensor("qkvb_v", [1, H], BF16)
        at_sb = arB.sbuf_tensor("at_sb", [128, NHT, 128], BF16)
        rwt = arB.sbuf_tensor("rwt", [128, NHT, 16], BF16)
        z_bf = arB.sbuf_tensor("z_bf", [128, TPC], BF16)
        logits_sb = arB.sbuf_tensor("logits_sb", [16, TPC], F32)
        expv = arB.sbuf_tensor("expv", [4, TPC], F32)

        recipZ = arB.sbuf_tensor("recipZ", [1, TPC], F32)
        alphaU = arB.sbuf_tensor("alphaU", [16, TPC], F32)
        alpha_bf = arB.sbuf_tensor("alpha_bf", [128, TPC], BF16)
        iota16x4 = arB.sbuf_tensor("iota16x4", [16, 4], F32)
        Rmat = arB.sbuf_tensor("Rmat", [16, 128], F32)
        identF = arB.sbuf_tensor("identF", [16, 16], F32)
        onehotT = arB.sbuf_tensor("onehotT", [4, 16], F32)
        onehot = arB.sbuf_tensor("onehot", [16, 4], F32)
        act_row_i = arB.sbuf_tensor("act_row_i", [1, 4], I32)
        act_row_f = arB.sbuf_tensor("act_row_f", [1, 4], F32)
        prior_sb = arB.sbuf_tensor("prior_sb", [16, 1], F32)
        rb_sb = arB.sbuf_tensor("rb_sb", [16, 1], F32)
        logp = arB.sbuf_tensor("logp", [4, 1], F32)
        arB.allocate()

        @blk.gpsimd
        def _(gp):
            gp.memset(ident[:, :], 0.0)
            gp.affine_select(out=ident[:, :], in_=ident[:, :],
                             compare_op=ALU.not_equal, fill=1.0, base=0,
                             pattern=[[-1, 128]], channel_multiplier=1)
            gp.memset(ones_row_f[:, :], 1.0)
            gp.memset(ones_col_f[:, :], 1.0)
            gp.memset(ones_row_bf[:, :], 1.0)
            gp.memset(eps12[:, :], EPS_PRIOR)
            gp.memset(ones_col_bf[:, :], 1.0)
            gp.iota(iota16x4[:, :], [[0, 4]], base=0, channel_multiplier=1,
                    allow_small_or_imprecise_dtypes=True)
            gp.memset(identF[:, :], 0.0)
            gp.affine_select(out=identF[:, :], in_=identF[:, :],
                             compare_op=ALU.not_equal, fill=1.0, base=0,
                             pattern=[[-1, 16]], channel_multiplier=1)
            gp.memset(Rmat[:, :], 0.0)
            ms["consts"] = sW.inc(gp.affine_select(
                out=Rmat[:, :], in_=Rmat[:, :],
                compare_op=ALU.not_equal, fill=1.0, base=0,
                pattern=[[1, 16], [0, 8]], channel_multiplier=-1))
            sW.dma(gp.dma_start(out=qkvb_v[:, :],
                                in_=qkvb.ap()[8192:12288].rearrange("(one c) -> one c", one=1)))
            ms["bias_loads"] = sW.dma(gp.dma_start(
                out=act_row_i[:, :],
                in_=active.ap().rearrange("(one c) -> one c", one=1)))

        @blk.sync
        def _(sy):
            sy.dma_start(out=qkvb_fm[:, :],
                         in_=qkvb.ap()[0:8192].rearrange("(ct p) -> p ct", p=128),
                         allow_slow_non_contiguous=True).then_inc(sLx.h, 16)
            sy.dma_start(out=prior_sb[:, :],
                         in_=prior.ap().rearrange("(p one) -> p one", one=1),
                         allow_slow_non_contiguous=True).then_inc(sLx.h, 16)
            sy.dma_start(out=rb_sb[:, :],
                         in_=rb.ap().rearrange("(p one) -> p one", one=1),
                         allow_slow_non_contiguous=True).then_inc(sLx.h, 16)
            sLx.n += 48
            ms["smalls_sync"] = sLx.n

        xflat = x_in.ap().rearrange("b s h -> (b s) h")
        for tt in range(NT):
            xb = xfm_t[tt % 2]

            @blk.sync
            def _(sy, tt=tt):
                if tt > 0:
                    sy.wait_ge(sV.h, ms[f"xbf_{tt - 1}"])
                ms[f"xload_{tt}"] = sLx.dma(
                    sy.dma_start(out=x_tile[:, :], in_=xflat[128 * tt:128 * (tt + 1), :]))

            @blk.vector
            def _(ve, tt=tt):
                ve.wait_ge(sLx.h, ms[f"xload_{tt}"])
                if tt > 0:
                    ve.wait_ge(sT.h, ms[f"tp_{tt - 1}_3"])
                ms[f"xbf_{tt}"] = sV.inc(ve.tensor_copy(x_bf[:, :], x_tile[:, :]))

            for g in range(4):
                pslo = (g % 2) * 1024

                @blk.tensor
                def _(t, tt=tt, g=g, pslo=pslo):
                    t.wait_ge(sV.h, ms[f"xbf_{tt}"])
                    prev = 4 * tt + g - 2
                    if prev >= 0:
                        t.wait_ge(sV.h, ms[f"tpev_g{prev}"])
                    for j in range(8):
                        i = g * 8 + j
                        ins = t.transpose(psT[:, pslo + 128 * j: pslo + 128 * (j + 1)],
                                          x_bf[:, 128 * i:128 * (i + 1)], ident[:, :])
                    ms[f"tp_{tt}_{g}"] = sT.inc(ins)

                @blk.vector
                def _(ve, tt=tt, g=g, pslo=pslo, xb=xb):
                    ve.wait_ge(sT.h, ms[f"tp_{tt}_{g}"])
                    if tt >= 1 and g == 0:
                        ve.wait_ge(sT.h, ms[f"stats_{tt - 1}"])
                        ve.wait_ge(sV.h, ms[f"xnw_{tt - 1}"])
                    ms[f"tpev_g{4 * tt + g}"] = sV.inc(ve.tensor_copy(
                        xb[:, 8 * g:8 * g + 8, :],
                        psT[:, pslo:pslo + 1024].rearrange("p (j t) -> p j t", j=8)))

            @blk.vector
            def _(ve, tt=tt, xb=xb):
                ve.wait_ge(sV.h, ms[f"tpev_g{4 * tt + 3}"])
                if tt > 0:
                    ve.wait_ge(sT.h, ms[f"stats_{tt - 1}"])
                ms[f"sq_{tt}"] = sV.inc(ve.tensor_tensor(
                    sq_t[:, :, :], xb[:, :, :], xb[:, :, :], ALU.mult))

            @blk.tensor
            def _(t, tt=tt, xb=xb):
                t.wait_ge(sV.h, ms[f"sq_{tt}"])
                if tt > 0:
                    t.wait_ge(sV.h, ms[f"stev_{tt - 1}"])
                for i in range(NHT):
                    t.matmul(psA[0:1, 0:128], ones_col_bf[:, :], xb[:, i, :],
                             start=(i == 0), stop=(i == NHT - 1))
                for i in range(NHT):
                    ins = t.matmul(psB[0:1, 0:128], ones_col_bf[:, :], sq_t[:, i, :],
                                   start=(i == 0), stop=(i == NHT - 1))
                ms[f"stats_{tt}"] = sT.inc(ins)

            @blk.vector
            def _(ve, tt=tt):
                ve.wait_ge(sT.h, ms[f"stats_{tt}"])
                ve.tensor_scalar(st["m"][:, :], psA[0:1, 0:128], 1.0 / H, None, ALU.mult)
                ve.tensor_scalar(st["A"][:, :], psB[0:1, 0:128], 1.0 / H, None, ALU.mult)
                ve.tensor_tensor(st["B"][:, :], st["m"][:, :], st["m"][:, :], ALU.mult)
                ve.tensor_tensor(st["V0"][:, :], st["A"][:, :], st["B"][:, :], ALU.subtract)
                ms[f"veps_{tt}"] = sV.inc(ve.tensor_scalar(
                    st["Veps"][:, :], st["V0"][:, :], EPS_LN, None, ALU.add))

            @blk.scalar
            def _(sc, tt=tt):
                sc.wait_ge(sV.h, ms[f"veps_{tt}"])
                sc.activation(st["s"][:, :], st["Veps"][:, :], AF.Sqrt)
                ms[f"sqrt_{tt}"] = sE.inc(sc.drain())

            @blk.vector
            def _(ve, tt=tt):
                ve.wait_ge(sE.h, ms[f"sqrt_{tt}"])
                ve.reciprocal(st["r0"][:, :], st["s"][:, :])
                ve.tensor_tensor(st["t2"][:, :], st["r0"][:, :], st["r0"][:, :], ALU.mult)
                ve.tensor_tensor(st["w"][:, :], st["Veps"][:, :], st["t2"][:, :], ALU.mult)
                ve.tensor_scalar(st["u"][:, :], st["w"][:, :], -0.5, 1.5, ALU.mult, ALU.add)
                ve.tensor_tensor(st["r"][:, :], st["r0"][:, :], st["u"][:, :], ALU.mult)
                ve.tensor_scalar(st["mneg"][:, :], st["m"][:, :], -1.0, None, ALU.mult)
                ms[f"rn_{tt}"] = sV.inc(ve.tensor_tensor(
                    st["negb"][:, :], st["mneg"][:, :], st["r"][:, :], ALU.mult))

            @blk.tensor
            def _(t, tt=tt):
                t.wait_ge(sV.h, ms[f"rn_{tt}"])
                t.matmul(psA[:, 0:128], ones_row_f[0:1, :], st["r"][:, :],
                         start=True, stop=True)
                ms[f"repmm0_{tt}"] = sT.inc(t.matmul(
                    psB[:, 0:128], ones_row_f[0:1, :], st["negb"][:, :],
                    start=True, stop=True))

            @blk.vector
            def _(ve, tt=tt, xb=xb):
                ve.wait_ge(sT.h, ms[f"repmm0_{tt}"])
                ve.tensor_copy(rrep_sb[:, :], psA[:, 0:128])
                ms[f"stev_{tt}"] = sV.inc(ve.tensor_copy(nrep_sb[:, :], psB[:, 0:128]))
                for i in range(NHT):
                    ve.tensor_tensor(xn_fm[:, i, 128 * tt:128 * (tt + 1)],
                                     xb[:, i, :], rrep_sb[:, :], ALU.mult)
                    ins = ve.tensor_tensor(xn_fm[:, i, 128 * tt:128 * (tt + 1)],
                                           xn_fm[:, i, 128 * tt:128 * (tt + 1)],
                                           nrep_sb[:, :], ALU.add)
                ms[f"xnw_{tt}"] = sV.inc(ins)

        ms["xn_all"] = ms["xnw_7"]

        @blk.scalar
        def _(sc):
            sc.wait_ge(sV.h, ms["xn_all"])
            ms["xn_spill"] = sSt.dma(sc.dma_start(out=xn_sp.ap(), in_=xn_fm[:, :, :]))

        @blk.sync
        def _(sy):
            sy.wait_ge(sV.h, ms["xbf_7"])
            for e in range(16):
                sy.dma_start(out=x_tile[8 * e:8 * (e + 1), :],
                             in_=lb.ap()[e].rearrange("o r -> r o"),
                             allow_slow_non_contiguous=True).then_inc(sLx.h, 16)
            sLx.n += 16 * 16
            ms["btT_f32"] = sLx.n

        if trunc == 0:
            @blk.sync
            def _(sy):
                sy.wait_ge(sSt.h, ms["xn_spill"])
                sy.wait_ge(sLx.h, ms["btT_f32"])
                sy.dma_start(out=d_xn.ap(), in_=xn_sp.ap()).then_inc(sOut.h, 16)
                sy.dma_start(out=d_zw.ap(), in_=zw[:, :]).then_inc(sOut.h, 16)
                sy.dma_start(out=d_bt.ap()[:, 0:128], in_=ident[:, :]).then_inc(sOut.h, 16)
                sy.wait_ge(sOut.h, 48)
            arB.free()
            arA.free()
            return nc

        kq_slots = [psW0[:, 0:512], psW0[:, 512:1024], psW1[:, 0:512], psW1[:, 512:1024]]
        kq_unit = 0
        v_unit = 0
        pan_seq = 0
        panels = [("K", p) for p in range(NPAN)] + [("V", p) for p in range(NPAN)] + \
                 [("Q", p) for p in range(NPAN)]

        for kind, p in panels:
            pb = pan_seq % 2
            col0 = {"Q": 0, "K": H, "V": 2 * H}[kind] + PW * p

            @blk.gpsimd
            def _(gp, pan_seq=pan_seq, pb=pb, col0=col0):
                if pan_seq >= 2:
                    gp.wait_ge(sT.h, ms[f"pan_used_{pan_seq - 2}"])
                ms[f"pan_{pan_seq}"] = sW.dma(gp.dma_start(
                    out=wpan[pb][:, :, :],
                    in_=qkvw.ap()[:, col0:col0 + PW].rearrange("(i p) c -> p i c", p=128)))

            if kind in ("K", "Q"):
                stage = k_stage if kind == "K" else q_stage
                dst = k_in if kind == "K" else q_sp
                for ct in range(2):
                    ctg = 2 * p + ct
                    for th in range(2):
                        slot = kq_slots[kq_unit % 4]

                        @blk.tensor
                        def _(t, pan_seq=pan_seq, pb=pb, ct=ct, th=th, slot=slot, u=kq_unit):
                            t.wait_ge(sW.h, ms[f"pan_{pan_seq}"])
                            t.wait_ge(sV.h, ms["xn_all"])
                            if u >= 4:
                                t.wait_ge(sV.h, ms[f"kqev_{u - 4}"])
                            for i in range(NHT):
                                ins = t.matmul(slot, wpan[pb][:, i, 128 * ct:128 * (ct + 1)],
                                               xn_fm[:, i, 512 * th:512 * (th + 1)],
                                               start=(i == 0), stop=(i == NHT - 1))
                            ms[f"kqmm_{u}"] = sT.inc(ins)

                        @blk.vector
                        def _(ve, u=kq_unit, th=th, ctg=ctg, slot=slot, stage=stage, kind=kind):
                            if u == 0:
                                ve.wait_ge(sLx.h, ms["smalls_sync"])
                            ve.wait_ge(sT.h, ms[f"kqmm_{u}"])
                            key = f"{kind}st_{ctg - 2}"
                            if key in ms:
                                ve.wait_ge(sSt.h, ms[key])
                            boff = (0 if kind == "Q" else 32) + ctg
                            ms[f"kqev_{u}"] = sV.inc(ve.tensor_scalar(
                                stage[:, ctg % 2, 512 * th:512 * (th + 1)], slot,
                                qkvb_fm[:, boff:boff + 1], None, ALU.add))
                        kq_unit += 1

                    @blk.scalar
                    def _(sc, ctg=ctg, stage=stage, dst=dst, kind=kind, u=kq_unit):
                        sc.wait_ge(sV.h, ms[f"kqev_{u - 1}"])
                        ms[f"{kind}st_{ctg}"] = sSt.dma(sc.dma_start(
                            out=dst.ap()[128 * ctg:128 * (ctg + 1), :],
                            in_=stage[:, ctg % 2, :]))
                ms[f"pan_used_{pan_seq}"] = ms[f"kqmm_{kq_unit - 1}"]
            else:
                for tt in range(NT):
                    vps = psA if v_unit % 2 == 0 else psB

                    @blk.tensor
                    def _(t, pan_seq=pan_seq, pb=pb, tt=tt, vps=vps, u=v_unit, col0=col0):
                        t.wait_ge(sW.h, ms[f"pan_{pan_seq}"])
                        t.wait_ge(sW.h, ms["bias_loads"])
                        t.wait_ge(sV.h, ms["xn_all"])
                        if u >= 2:
                            t.wait_ge(sV.h, ms[f"vev_{u - 2}"])
                        for i in range(NHT):
                            t.matmul(vps[:, 0:PW], xn_fm[:, i, 128 * tt:128 * (tt + 1)],
                                     wpan[pb][:, i, :], start=(i == 0), stop=False)
                        voff = col0 - 2 * H
                        ms[f"vmm_{u}"] = sT.inc(t.matmul(
                            vps[:, 0:PW], ones_row_bf[:1, :],
                            qkvb_v[0:1, voff:voff + PW], start=False, stop=True))

                    @blk.vector
                    def _(ve, u=v_unit, tt=tt, vps=vps, p=p):
                        ve.wait_ge(sT.h, ms[f"vmm_{u}"])
                        key = f"Vst_{p - 1}"
                        if key in ms:
                            ve.wait_ge(sSt.h, ms[key])
                        ms[f"vev_{u}"] = sV.inc(ve.tensor_copy(v_stage[:, tt, :], vps[:, 0:PW]))
                    v_unit += 1

                @blk.scalar
                def _(sc, p=p, u=v_unit):
                    sc.wait_ge(sV.h, ms[f"vev_{u - 1}"])
                    for hh in range(2):
                        head = 2 * p + hh
                        ins = sc.dma_start(
                            out=v_in.ap()[head].rearrange("(tt s) d -> s tt d", s=128),
                            in_=v_stage[:, :, 128 * hh:128 * (hh + 1)])
                        sSt.dma(ins)
                    ms[f"Vst_{p}"] = sSt.n
                ms[f"pan_used_{pan_seq}"] = ms[f"vmm_{v_unit - 1}"]

            pan_seq += 1
            if kind == "V" and p == NPAN - 1:
                ms["kv_goal"] = sSt.n

        if trunc in (10, 11):
            @blk.sync
            def _(sy):
                if trunc == 11:
                    sy.wait_ge(sCC.h, ms["ag"])
                sy.wait_ge(sSt.h, ms["Qst_31"])
                sy.wait_ge(sV.h, ms["kqev_127"])
                sy.wait_ge(sLx.h, ms["btT_f32"])
                sy.dma_start(out=d_q.ap(), in_=q_sp.ap()).then_inc(sOut.h, 16)
                sy.dma_start(out=d_k.ap(), in_=k_in.ap()).then_inc(sOut.h, 16)
                sy.wait_ge(sOut.h, 32)
            arB.free()
            arA.free()
            return nc

        @blk.gpsimd
        def _(gp):
            gp.wait_ge(sT.h, ms["tp_7_3"])
            ms["p15_loads"] = sW.dma(gp.dma_start(
                out=x_bf[:, :], in_=la.ap().rearrange("e r h -> (e r) h")))

        for g in range(4):
            pslo = (g % 2) * 1024

            @blk.tensor
            def _(t, g=g, pslo=pslo):
                t.wait_ge(sW.h, ms["p15_loads"])
                if g < 2:
                    t.wait_ge(sV.h, ms["tpev_g31"])
                if g >= 2:
                    t.wait_ge(sV.h, ms[f"atev_{g - 2}"])
                for j in range(8):
                    i = g * 8 + j
                    ins = t.transpose(psT[:, pslo + 128 * j:pslo + 128 * (j + 1)],
                                      x_bf[:, 128 * i:128 * (i + 1)], ident[:, :])
                ms[f"attp_{g}"] = sT.inc(ins)

            @blk.vector
            def _(ve, g=g, pslo=pslo):
                ve.wait_ge(sT.h, ms[f"attp_{g}"])
                ms[f"atev_{g}"] = sV.inc(ve.tensor_copy(
                    at_sb[:, 8 * g:8 * g + 8, :],
                    psT[:, pslo:pslo + 1024].rearrange("p (j t) -> p j t", j=8)))

        @blk.gpsimd
        def _(gp):
            gp.wait_ge(sT.h, ms["attp_3"])
            ms["rw_load"] = sW.dma(gp.dma_start(out=x_bf[0:16, :], in_=rw.ap()))
            if trunc != 10:
                gp.wait_ge(sSt.h, ms["kv_goal"])
                sCC.inc(gp.collective_compute(
                    "AllGather", ALU.bypass, replica_groups=[list(range(NCORE))],
                    ins=[k_in.ap().opt()], outs=[k_g.ap().opt()]))
                ms["ag"] = sCC.inc(gp.collective_compute(
                    "AllGather", ALU.bypass, replica_groups=[list(range(NCORE))],
                    ins=[v_in.ap().opt()], outs=[v_g.ap().opt()]))

        @blk.tensor
        def _(t):
            t.wait_ge(sV.h, ms["atev_2"])
            t.wait_ge(sW.h, ms["rw_load"])
            for i in range(NHT):
                ins = t.transpose(psT[:, 16 * i:16 * (i + 1)],
                                  x_bf[0:16, 128 * i:128 * (i + 1)], ident[0:16, 0:16])
            ms["rwtp"] = sT.inc(ins)

        @blk.vector
        def _(ve):
            ve.wait_ge(sT.h, ms["rwtp"])
            ms["rwev"] = sV.inc(ve.tensor_copy(
                rwt[:, :, :], psT[:, 0:512].rearrange("p (i e) -> p i e", i=32)))

        @blk.vector
        def _(ve):
            ve.wait_ge(sLx.h, ms["btT_f32"])
            ms["btT_done"] = sV.inc(ve.tensor_copy(btT[:, :], x_tile[:, :]))

        @blk.vector
        def _(ve):
            ve.wait_ge(sW.h, ms["bias_loads"])
            ve.wait_ge(sW.h, ms["consts"])
            ve.wait_ge(sLx.h, ms["smalls_sync"])
            ms["oh1"] = sV.inc(ve.tensor_copy(act_row_f[:, :], act_row_i[:, :]))

        if trunc == 12:
            @blk.sync
            def _(sy):
                sy.wait_ge(sV.h, ms["atev_3"])
                sy.wait_ge(sV.h, ms["rwev"])
                sy.wait_ge(sV.h, ms["btT_done"])
                sy.wait_ge(sV.h, ms["oh1"])
                sy.wait_ge(sCC.h, ms["ag"])
            arB.free()
            arA.free()
            return nc

        for th in range(2):
            @blk.tensor
            def _(t, th=th):
                t.wait_ge(sV.h, ms["atev_3"])
                if th == 0:
                    t.wait_ge(sV.h, ms[f"vev_{v_unit - 1}"])
                    t.wait_ge(sV.h, ms[f"vev_{v_unit - 2}"])
                ps = psA if th == 0 else psB
                for i in range(NHT):
                    ins = t.matmul(ps[:, 0:512], at_sb[:, i, :],
                                   xn_fm[:, i, 512 * th:512 * (th + 1)],
                                   start=(i == 0), stop=(i == NHT - 1))
                ms[f"zmm_{th}"] = sT.inc(ins)

            @blk.vector
            def _(ve, th=th):
                ve.wait_ge(sT.h, ms[f"zmm_{th}"])
                ps = psA if th == 0 else psB
                ms[f"zev_{th}"] = sV.inc(ve.tensor_copy(
                    z_bf[:, 512 * th:512 * (th + 1)], ps[:, 0:512]))

        for th in range(2):
            @blk.tensor
            def _(t, th=th):
                if th == 0:
                    t.wait_ge(sV.h, ms["kqev_127"])
                    t.wait_ge(sV.h, ms["rwev"])
                if th > 0:
                    t.wait_ge(sV.h, ms[f"lgev_{th - 1}"])
                for i in range(NHT):
                    ins = t.matmul(psW0[0:16, 0:512], rwt[:, i, :],
                                   xn_fm[:, i, 512 * th:512 * (th + 1)],
                                   start=(i == 0), stop=(i == NHT - 1))
                ms[f"lgmm_{th}"] = sT.inc(ins)

            @blk.vector
            def _(ve, th=th):
                ve.wait_ge(sT.h, ms[f"lgmm_{th}"])
                ms[f"lgev_{th}"] = sV.inc(ve.tensor_scalar(
                    logits_sb[:, 512 * th:512 * (th + 1)], psW0[0:16, 0:512],
                    rb_sb[:, :1], None, ALU.add))

        if trunc == 13:
            @blk.sync
            def _(sy):
                sy.wait_ge(sE.h, ms["zev_1"])
                sy.wait_ge(sE.h, ms["lgev_1"])
                sy.wait_ge(sCC.h, ms["ag"])
            arB.free()
            arA.free()
            return nc

        @blk.tensor
        def _(t):
            t.wait_ge(sV.h, ms["lgev_1"])
            t.wait_ge(sV.h, ms["oh1"])
            ms["actrep"] = sT.inc(t.matmul(psW0[0:16, 0:4], ones_row_f[0:1, 0:16],
                                           act_row_f[0:1, :], start=True, stop=True))

        @blk.vector
        def _(ve):
            ve.wait_ge(sT.h, ms["actrep"])
            ms["oh2"] = sV.inc(ve.tensor_tensor(
                onehot[:, :], iota16x4[:, :], psW0[0:16, 0:4], ALU.is_equal))

        @blk.tensor
        def _(t):
            t.wait_ge(sV.h, ms["oh2"])
            t.wait_ge(sV.h, ms["zev_0"])
            ms["ohTmm"] = sT.inc(t.transpose(psA[0:4, 0:16], onehot[:, :],
                                             identF[:, :]))

        @blk.vector
        def _(ve):
            ve.wait_ge(sT.h, ms["ohTmm"])
            ms["ohT"] = sV.inc(ve.tensor_copy(onehotT[:, :], psA[0:4, 0:16]))

        @blk.tensor
        def _(t):
            t.wait_ge(sV.h, ms["oh2"])
            ms["prmm"] = sT.inc(t.matmul(psW0[0:4, 0:1], onehot[:, :], prior_sb[:, :],
                                         start=True, stop=True))

        @blk.scalar
        def _(sc):
            sc.wait_ge(sT.h, ms["prmm"])
            sc.activation(logp[:, :], psW0[0:4, 0:1], AF.Ln, bias=eps12[0:4, :1])
            ms["logp"] = sE.inc(sc.drain())

        if trunc == 14:
            @blk.sync
            def _(sy):
                sy.wait_ge(sE.h, ms["logp"])
                sy.wait_ge(sCC.h, ms["ag"])
            arB.free()
            arA.free()
            return nc

        for th in range(2):
            @blk.tensor
            def _(t, th=th):
                t.wait_ge(sE.h, ms["logp"])
                if th > 0:
                    t.wait_ge(sE.h, ms[f"expv_{th - 1}"])
                ms[f"lsmm_{th}"] = sT.inc(t.matmul(
                    psW1[0:4, 0:512], onehot[:, :],
                    logits_sb[:, 512 * th:512 * (th + 1)], start=True, stop=True))

            @blk.scalar
            def _(sc, th=th):
                sc.wait_ge(sT.h, ms[f"lsmm_{th}"])
                sc.activation(expv[:, 512 * th:512 * (th + 1)], psW1[0:4, 0:512],
                              AF.Exp, bias=logp[:, :1])
                ms[f"expv_{th}"] = sE.inc(sc.drain())

        if trunc == 15:
            @blk.sync
            def _(sy):
                sy.wait_ge(sE.h, ms["expv_1"])
                sy.wait_ge(sCC.h, ms["ag"])
            arB.free()
            arA.free()
            return nc

        for th in range(2):
            @blk.tensor
            def _(t, th=th):
                t.wait_ge(sE.h, ms[f"expv_{th}"])
                if th > 0:
                    t.wait_ge(sV.h, ms[f"rz_{th - 1}"])
                ms[f"szmm_{th}"] = sT.inc(t.matmul(
                    psW0[0:1, 0:512], ones_col_f[0:4, :],
                    expv[:, 512 * th:512 * (th + 1)], start=True, stop=True))

            @blk.vector
            def _(ve, th=th):
                ve.wait_ge(sT.h, ms[f"szmm_{th}"])
                ms[f"rz_{th}"] = sV.inc(ve.reciprocal(
                    recipZ[0:1, 512 * th:512 * (th + 1)], psW0[0:1, 0:512]))

        if trunc == 16:
            @blk.sync
            def _(sy):
                sy.wait_ge(sV.h, ms["rz_1"])
                sy.wait_ge(sCC.h, ms["ag"])
            arB.free()
            arA.free()
            return nc

        for th in range(2):
            @blk.tensor
            def _(t, th=th):
                t.wait_ge(sV.h, ms[f"rz_{th}"])
                t.wait_ge(sE.h, ms["expv_1"])
                if th > 0:
                    t.wait_ge(sV.h, ms[f"an_{th - 1}"])
                ms[f"rzr_{th}"] = sT.inc(t.matmul(
                    psW1[0:4, 0:512], ones_row_f[0:1, 0:4],
                    recipZ[0:1, 512 * th:512 * (th + 1)], start=True, stop=True))

            @blk.vector
            def _(ve, th=th):
                ve.wait_ge(sT.h, ms[f"rzr_{th}"])
                ms[f"an_{th}"] = sV.inc(ve.tensor_tensor(
                    expv[:, 512 * th:512 * (th + 1)],
                    expv[:, 512 * th:512 * (th + 1)], psW1[0:4, 0:512], ALU.mult))

        if trunc == 17:
            @blk.sync
            def _(sy):
                sy.wait_ge(sV.h, ms["an_1"])
                sy.wait_ge(sCC.h, ms["ag"])
            arB.free()
            arA.free()
            return nc

        for th in range(2):
            @blk.tensor
            def _(t, th=th):
                t.wait_ge(sV.h, ms[f"an_{th}"])
                t.wait_ge(sV.h, ms["ohT"])
                if th > 0:
                    t.wait_ge(sV.h, ms[f"auev_{th - 1}"])
                ms[f"aumm_{th}"] = sT.inc(t.matmul(
                    psW0[0:16, 0:512], onehotT[:, :],
                    expv[:, 512 * th:512 * (th + 1)], start=True, stop=True))

            @blk.vector
            def _(ve, th=th):
                ve.wait_ge(sT.h, ms[f"aumm_{th}"])
                ms[f"auev_{th}"] = sV.inc(ve.tensor_copy(
                    alphaU[:, 512 * th:512 * (th + 1)], psW0[0:16, 0:512]))

        if trunc == 18:
            @blk.sync
            def _(sy):
                sy.wait_ge(sV.h, ms["auev_1"])
                sy.wait_ge(sCC.h, ms["ag"])
            arB.free()
            arA.free()
            return nc

        for th in range(2):
            @blk.tensor
            def _(t, th=th):
                t.wait_ge(sV.h, ms["auev_1"])
                if th > 0:
                    t.wait_ge(sV.h, ms[f"arep_{th - 1}"])
                ms[f"armm_{th}"] = sT.inc(t.matmul(
                    psW0[:, 0:512], Rmat[:, :],
                    alphaU[:, 512 * th:512 * (th + 1)], start=True, stop=True))

            @blk.vector
            def _(ve, th=th):
                ve.wait_ge(sT.h, ms[f"armm_{th}"])
                ms[f"arep_{th}"] = sV.inc(ve.tensor_scalar(
                    alpha_bf[:, 512 * th:512 * (th + 1)], psW0[:, 0:512],
                    LORA_SCALING, None, ALU.mult))

        if trunc == 19:
            @blk.sync
            def _(sy):
                sy.wait_ge(sV.h, ms["arep_1"])
                sy.wait_ge(sCC.h, ms["ag"])
            arB.free()
            arA.free()
            return nc

        @blk.vector
        def _(ve):
            ve.wait_ge(sV.h, ms["zev_1"])
            ms["zw_done"] = sV.inc(ve.tensor_tensor(zw[:, :], z_bf[:, :], alpha_bf[:, :],
                                                    ALU.mult))

        if trunc == 1:
            @blk.sync
            def _(sy):
                sy.wait_ge(sCC.h, ms["ag"])
                sy.wait_ge(sV.h, ms["zw_done"])
                sy.wait_ge(sV.h, ms["btT_done"])
                sy.wait_ge(sSt.h, ms["Qst_31"])
                sy.wait_ge(sV.h, ms["kqev_127"])
                for dst, srcap in [(d_q, q_sp.ap()), (d_k, k_in.ap()), (d_v, v_in.ap()),
                                   (d_xn, xn_sp.ap())]:
                    sy.dma_start(out=dst.ap(), in_=srcap).then_inc(sOut.h, 16)
                sy.dma_start(out=d_zw.ap(), in_=zw[:, :]).then_inc(sOut.h, 16)
                sy.dma_start(out=d_alpha.ap(), in_=alpha_bf[:, :]).then_inc(sOut.h, 16)
                sy.dma_start(out=d_z.ap(), in_=z_bf[:, :]).then_inc(sOut.h, 16)
                sy.dma_start(out=d_lg.ap(), in_=logits_sb[:, :]).then_inc(sOut.h, 16)
                for kk, (r0_, c0_, srcp) in enumerate([
                        (0, 0, onehotT[:, :]),
                        (0, 20, act_row_f[:, :]),
                        (16, 0, onehot[:, :])]):
                    sh = srcp.shape
                    sy.dma_start(out=d_lg2.ap()[r0_:r0_ + sh[0], c0_:c0_ + sh[1]],
                                 in_=srcp,
                                 allow_slow_non_contiguous=True).then_inc(sOut.h, 16)
                sy.wait_ge(sOut.h, 16 * 11)
            arB.free()
            arA.free()
            return nc

        arB.free()
        arA.free()
        arCtx = bass.AutoArena(nc)
        ctx_fm = arCtx.sbuf_tensor("ctx_fm", [128, NH, TPC], BF16)
        arCtx.allocate()
        arC = bass.AutoArena(nc)
        kbuf = [arC.sbuf_tensor(f"kbuf{i}", [128, 8, 2, 512], BF16) for i in range(2)]
        vbuf = [arC.sbuf_tensor(f"vbuf{i}", [128, 64, 128], BF16) for i in range(2)]
        qbuf = [arC.sbuf_tensor(f"qbuf{i}", [128, TPC], BF16) for i in range(2)]
        den_acc = arC.sbuf_tensor("den_acc", [128, TPC], F32)
        probsT = [arC.sbuf_tensor(f"probsT{i}", [128, 1024], BF16) for i in range(2)]
        recip = arC.sbuf_tensor("recip", [1, 512], F32)
        rep_sb = arC.sbuf_tensor("rep_sb", [128, 512], F32)
        arC.allocate()

        pair = 0
        for h in range(NH):
            @blk.sync
            def _(sy, h=h):
                if h == 0:
                    sy.wait_ge(sCC.h, ms["ag"])
                    sy.wait_ge(sSt.h, ms["Qst_31"])
                    sy.wait_ge(sV.h, ms["zw_done"])
                    sy.wait_ge(sT.h, ms["armm_1"])
                if h >= 2:
                    sy.wait_ge(sT.h, ms[f"hd_mm_{h - 2}"])
                hb = h % 2
                for r in range(NCORE):
                    sKV.dma(sy.dma_start(
                        out=kbuf[hb][:, r, :, :],
                        in_=k_g.ap()[4096 * r + 128 * h:4096 * r + 128 * (h + 1), :]))
                for r in range(NCORE):
                    for b in range(2):
                        sKV.dma(sy.dma_start(
                            out=vbuf[hb][:, r * 8 + b * 4:r * 8 + b * 4 + 4, :],
                            in_=v_g.ap()[NH * r + h, 512 * b:512 * (b + 1), :]
                            .rearrange("(j s) d -> s j d", s=128)))
                ms[f"hload_{h}"] = sKV.dma(sy.dma_start(
                    out=qbuf[hb][:, :], in_=q_sp.ap()[128 * h:128 * (h + 1), :]))

            hb = h % 2
            for b in range(2):
                ctx_ps = psA if b == 0 else psB
                dslice = den_acc[:, 512 * b:512 * (b + 1)]

                def emit_scores(j, pair):
                    psW = psW0 if j % 2 == 0 else psW1

                    @blk.tensor
                    def _(t, j=j, pair=pair, psW=psW):
                        if j == 0 and b == 0:
                            t.wait_ge(sKV.h, ms[f"hload_{h}"])
                        if pair == 0:
                            t.wait_ge(sV.h, ms["arep_1"])
                        if pair == 1:
                            t.wait_ge(sV.h, ms["an_1"])
                        if pair >= 2:
                            t.wait_ge(sE.h, ms[f"exp_{pair - 2}"])
                        for kk, kt in ((0, 2 * j), (1, 2 * j + 1)):
                            r, jj = kt // 4, kt % 4
                            ins = t.matmul(psW[:, 512 * kk:512 * (kk + 1)],
                                           kbuf[hb][:, r, b, 128 * jj:128 * (jj + 1)],
                                           qbuf[hb][:, 512 * b:512 * (b + 1)],
                                           start=True, stop=True)
                        ms[f"sc_{pair}"] = sT.inc(ins)

                    @blk.scalar
                    def _(sc, j=j, pair=pair, psW=psW):
                        sc.wait_ge(sT.h, ms[f"sc_{pair}"])
                        if pair >= 2:
                            sc.wait_ge(sT.h, ms[f"ctx_{pair - 2}"])
                            sc.wait_ge(sV.h, ms[f"den_{pair - 2}"])
                        ins = sc.activation(probsT[j % 2][:, :], psW[:, :], AF.Exp,
                                            scale=RSCALE)
                        if j == 15:
                            ins = sc.drain()
                        ms[f"exp_{pair}"] = sE.inc(ins)

                def emit_ctx_den(j, pair):
                    @blk.tensor
                    def _(t, j=j, pair=pair):
                        t.wait_ge(sE.h, ms[f"exp_{pair + 1}"] if j < 15 else ms[f"exp_{pair}"])
                        if j == 0:
                            key = f"ctxev_{(2 * h + b) - 1}"
                            if key in ms:
                                t.wait_ge(sV.h, ms[key])
                        for kk, kt in ((0, 2 * j), (1, 2 * j + 1)):
                            r, jj = kt // 4, kt % 4
                            vkt = r * 8 + b * 4 + jj
                            ins = t.matmul(ctx_ps[:, :], vbuf[hb][:, vkt, :],
                                           probsT[j % 2][:, 512 * kk:512 * (kk + 1)],
                                           start=(j == 0 and kk == 0),
                                           stop=(j == 15 and kk == 1))
                        ms[f"ctx_{pair}"] = sT.inc(ins)

                    @blk.vector
                    def _(ve, j=j, pair=pair, dslice=dslice):
                        ve.wait_ge(sE.h, ms[f"exp_{pair + 1}"] if j < 15 else ms[f"exp_{pair}"])
                        if j == 0:
                            ve.tensor_copy(dslice, probsT[0][:, 0:512])
                        else:
                            ve.tensor_tensor(dslice, dslice, probsT[j % 2][:, 0:512], ALU.add)
                        ms[f"den_{pair}"] = sV.inc(ve.tensor_tensor(
                            dslice, dslice, probsT[j % 2][:, 512:1024], ALU.add))

                base_pair = pair
                for j in range(16):
                    emit_scores(j, base_pair + j)
                    if j >= 1:
                        emit_ctx_den(j - 1, base_pair + j - 1)
                emit_ctx_den(15, base_pair + 15)
                pair = base_pair + 16

                dps = psB if b == 0 else psA

                @blk.tensor
                def _(t, h=h, b=b, dslice=dslice, dps=dps):
                    t.wait_ge(sV.h, ms[f"den_{pair - 1}"])
                    key = f"ctxev_{(2 * h + b) - 1}"
                    if key in ms:
                        t.wait_ge(sV.h, ms[key])
                    elif h == 0 and b == 0:
                        t.wait_ge(sV.h, ms["zev_1"])
                    ms[f"denmm_{2 * h + b}"] = sT.inc(t.matmul(
                        dps[0:1, 0:512], ones_col_f[:, :], dslice, start=True, stop=True))

                @blk.vector
                def _(ve, h=h, b=b, dps=dps):
                    ve.wait_ge(sT.h, ms[f"denmm_{2 * h + b}"])
                    ms[f"recip_{2 * h + b}"] = sV.inc(ve.reciprocal(recip[0:1, :],
                                                                    dps[0:1, 0:512]))

                @blk.tensor
                def _(t, h=h, b=b, dps=dps):
                    t.wait_ge(sV.h, ms[f"recip_{2 * h + b}"])
                    ms[f"repmm_{2 * h + b}"] = sT.inc(t.matmul(
                        dps[:, :], ones_row_f[0:1, :], recip[0:1, :], start=True, stop=True))

                @blk.vector
                def _(ve, h=h, b=b, ctx_ps=ctx_ps, dps=dps):
                    ve.wait_ge(sT.h, ms[f"repmm_{2 * h + b}"])
                    ve.tensor_copy(rep_sb[:, :], dps[:, :])
                    ms[f"ctxev_{2 * h + b}"] = sV.inc(ve.tensor_tensor(
                        ctx_fm[:, h, 512 * b:512 * (b + 1)], ctx_ps[:, :], rep_sb[:, :],
                        ALU.mult))
            ms[f"hd_mm_{h}"] = ms[f"repmm_{2 * h + 1}"]

        ms["p2_done"] = ms["ctxev_63"]

        if trunc == 2:
            @blk.sync
            def _(sy):
                sy.wait_ge(sV.h, ms["p2_done"])
                sy.dma_start(out=d_ctx.ap(), in_=ctx_fm[:, :, :]).then_inc(sOut.h, 16)
                sy.wait_ge(sOut.h, 16)
            arC.free()
            arCtx.free()
            return nc

        arC.free()
        arD = bass.AutoArena(nc)
        owp = [arD.sbuf_tensor(f"owp{i}", [128, NHT, PW], BF16) for i in range(2)]
        wbT = [arD.sbuf_tensor(f"wbT{i}", [128, NHT, PW], BF16) for i in range(2)]
        wb_stage = arD.sbuf_tensor("wb_stage", [128, 2, H], BF16)
        xnt = [arD.sbuf_tensor(f"xnt{i}", [128, NHT, 128], BF16) for i in range(2)]
        bias_comb = arD.sbuf_tensor("bias_comb", [1, H], BF16)
        bb_row = arD.sbuf_tensor("bb_row", [1, H], BF16)
        out_sb = [arD.sbuf_tensor(f"out_sb{i}", [128, PW], F32) for i in range(2)]
        xres = [arD.sbuf_tensor(f"xres{i}", [128, PW], F32) for i in range(2)]
        arD.allocate()

        @blk.gpsimd
        def _(gp):
            gp.wait_ge(sV.h, ms["p2_done"])
            sW.dma(gp.dma_start(out=bias_comb[:, :],
                                in_=ob.ap().rearrange("(one c) -> one c", one=1)))
            ms["biasld"] = sW.dma(gp.dma_start(
                out=bb_row[:, :], in_=bb.ap().rearrange("(one c) -> one c", one=1)))

        @blk.vector
        def _(ve):
            ve.wait_ge(sW.h, ms["biasld"])
            ms["biascomb"] = sV.inc(ve.tensor_tensor(bias_comb[:, :], bias_comb[:, :],
                                                     bb_row[:, :], ALU.add))

        for p3 in range(NPAN):
            pb = p3 % 2
            o0 = PW * p3

            @blk.gpsimd
            def _(gp, p3=p3, pb=pb, o0=o0):
                if p3 == 0:
                    gp.wait_ge(sV.h, ms["p2_done"])
                if p3 >= 1:
                    gp.wait_ge(sT.h, ms[f"wt_{8 * (p3 - 1) + 7}"])
                sW.dma(gp.dma_start(
                    out=owp[pb][:, :, :],
                    in_=ow.ap()[:, o0:o0 + PW].rearrange("(i q) c -> q i c", q=128)))
                ms[f"p3pan_{p3}"] = sW.dma(gp.dma_start(
                    out=wb_stage[:, :, :],
                    in_=wb.ap()[o0:o0 + PW, :].rearrange("(j q) h -> q j h", q=128)))

            for g in range(8):
                pslo = (g % 2) * 1024
                j2, ib = g // 4, (g % 4) * 8

                @blk.tensor
                def _(t, p3=p3, g=g, pslo=pslo, j2=j2, ib=ib):
                    t.wait_ge(sW.h, ms[f"p3pan_{p3}"])
                    gg = 8 * p3 + g
                    if gg >= 2:
                        t.wait_ge(sV.h, ms[f"wtev_{gg - 2}"])
                    if p3 >= 1 and g < 2:
                        pass
                    for j in range(8):
                        i = ib + j
                        ins = t.transpose(psT[:, pslo + 128 * j:pslo + 128 * (j + 1)],
                                          wb_stage[:, j2, 128 * i:128 * (i + 1)], ident[:, :])
                    ms[f"wt_{8 * p3 + g}"] = sT.inc(ins)

                @blk.vector
                def _(ve, p3=p3, g=g, pslo=pslo, j2=j2, ib=ib, pb=pb):
                    gg = 8 * p3 + g
                    ve.wait_ge(sT.h, ms[f"wt_{gg}"])
                    if g == 0 and p3 >= 2:
                        ve.wait_ge(sT.h, ms[f"p3used_{p3 - 2}"])
                    ms[f"wtev_{gg}"] = sV.inc(ve.tensor_copy(
                        wbT[pb][:, ib:ib + 8, 128 * j2:128 * (j2 + 1)],
                        psT[:, pslo:pslo + 1024].rearrange("p (j t) -> p j t", j=8)))

            for tt in range(NT):
                b, s0 = tt // 4, 128 * (tt % 4)
                out_ps = psA if tt % 2 == 0 else psB
                u3 = p3 * NT + tt

                @blk.sync
                def _(sy, p3=p3, tt=tt, b=b, s0=s0, o0=o0, u3=u3):
                    if u3 == 0:
                        sy.wait_ge(sSt.h, ms["xn_spill"])
                    if u3 >= 2:
                        sy.wait_ge(sT.h, ms[f"omm_{u3 - 2}"])
                    sKV.dma(sy.dma_start(out=xnt[tt % 2][:, :, :],
                                         in_=xn_sp.ap()[:, :, 128 * tt:128 * (tt + 1)]))
                    ms[f"p3ld_{u3}"] = sKV.dma(sy.dma_start(
                        out=xres[tt % 2][:, :], in_=x_in.ap()[b, s0:s0 + 128, o0:o0 + PW]))

                @blk.tensor
                def _(t, p3=p3, tt=tt, pb=pb, out_ps=out_ps, u3=u3, o0=o0):
                    t.wait_ge(sKV.h, ms[f"p3ld_{u3}"])
                    t.wait_ge(sV.h, ms[f"wtev_{8 * p3 + 7}"])
                    if u3 == 0:
                        t.wait_ge(sV.h, ms["biascomb"])
                        t.wait_ge(sV.h, ms["zw_done"])
                    if u3 >= 2:
                        t.wait_ge(sV.h, ms[f"oev_{u3 - 2}"])
                    for i in range(NHT):
                        t.matmul(out_ps[:, 0:PW], ctx_fm[:, i, 128 * tt:128 * (tt + 1)],
                                 owp[pb][:, i, :], start=(i == 0), stop=False)
                    for i in range(NHT):
                        t.matmul(out_ps[:, 0:PW], xnt[tt % 2][:, i, :], wbT[pb][:, i, :],
                                 start=False, stop=False)
                    t.matmul(out_ps[:, 0:PW], zw[:, 128 * tt:128 * (tt + 1)],
                             btT[:, o0:o0 + PW], start=False, stop=False)
                    ms[f"omm_{u3}"] = sT.inc(t.matmul(
                        out_ps[:, 0:PW], ones_row_bf[:1, :], bias_comb[0:1, o0:o0 + PW],
                        start=False, stop=True))

                @blk.vector
                def _(ve, tt=tt, out_ps=out_ps, u3=u3):
                    ve.wait_ge(sT.h, ms[f"omm_{u3}"])
                    if u3 >= 2:
                        ve.wait_ge(sOut.h, ms[f"ost_{u3 - 2}"])
                    ms[f"oev_{u3}"] = sV.inc(ve.tensor_tensor(
                        out_sb[tt % 2][:, :], out_ps[:, 0:PW], xres[tt % 2][:, :], ALU.add))

                @blk.sync
                def _(sy, tt=tt, b=b, s0=s0, o0=o0, u3=u3):
                    sy.wait_ge(sV.h, ms[f"oev_{u3}"])
                    ms[f"ost_{u3}"] = sOut.dma(sy.dma_start(
                        out=out.ap()[b, s0:s0 + 128, o0:o0 + PW], in_=out_sb[tt % 2][:, :]))

            ms[f"p3used_{p3}"] = ms[f"omm_{p3 * NT + NT - 1}"]

        @blk.sync
        def _(sy):
            sy.wait_ge(sOut.h, ms[f"ost_{NPAN * NT - 1}"])
            sy.wait_ge(sOut.h, 16 * NPAN * NT)

        arD.free()
        arCtx.free()

    return nc


_NC_CACHE = None


def kernel(L0_attn_x, L0_attn_attn_qkvw, L0_attn_attn_qkvb, L0_attn_attn_ow,
           L0_attn_attn_ob, L0_ffn_W_base, L0_ffn_b_base, L0_ffn_lora_A,
           L0_ffn_lora_B, L0_ffn_router_W, L0_ffn_router_b, L0_ffn_domain_prior,
           L0_ffn_active):
    global _NC_CACHE
    x = np.ascontiguousarray(np.asarray(L0_attn_x, dtype=np.float32))
    shared = {
        "qkvw": np.ascontiguousarray(np.asarray(L0_attn_attn_qkvw, np.float32)),
        "qkvb": np.ascontiguousarray(np.asarray(L0_attn_attn_qkvb, np.float32)),
        "ow": np.ascontiguousarray(np.asarray(L0_attn_attn_ow, np.float32)),
        "ob": np.ascontiguousarray(np.asarray(L0_attn_attn_ob, np.float32)),
        "wb": np.ascontiguousarray(np.asarray(L0_ffn_W_base, np.float32)),
        "bb": np.ascontiguousarray(np.asarray(L0_ffn_b_base, np.float32)),
        "la": np.ascontiguousarray(np.asarray(L0_ffn_lora_A, np.float32)),
        "lb": np.ascontiguousarray(np.asarray(L0_ffn_lora_B, np.float32)),
        "rw": np.ascontiguousarray(np.asarray(L0_ffn_router_W, np.float32)),
        "rb": np.ascontiguousarray(np.asarray(L0_ffn_router_b, np.float32)),
        "prior": np.ascontiguousarray(np.asarray(L0_ffn_domain_prior, np.float32)),
        "active": np.ascontiguousarray(np.asarray(L0_ffn_active, np.int32)),
    }
    if _NC_CACHE is None:
        _NC_CACHE = build_nc()
    nc = _NC_CACHE
    in_maps = []
    for c in range(NCORE):
        m = dict(shared)
        m["x"] = np.ascontiguousarray(x[:, SB_ * c:SB_ * (c + 1), :])
        in_maps.append(m)
    res = run_bass_kernel_spmd(nc, in_maps, core_ids=list(range(NCORE)))
    full = np.empty((2, NCORE * SB_, H), np.float32)
    for c in range(NCORE):
        full[:, SB_ * c:SB_ * (c + 1), :] = res.results[c]["out"]
    return full
